# version 01
# baseline (speedup 1.0000x reference)
from contextlib import ExitStack

import numpy as np
import ml_dtypes
import concourse.bass as bass
import concourse.mybir as mybir
from concourse.bass_utils import run_bass_kernel_spmd

F32 = mybir.dt.float32
BF16 = mybir.dt.bfloat16
AF = mybir.ActivationFunctionType
ALU = mybir.AluOpType

PHASES = 5
DBG = 0

D = 2048
S = 2048
NDC = 16
NSG = 4
SGS = 512
NEC = 16
DFF = 5632
NJS = 22
JSS = 256
NJC = 44
LN_EPS = 1e-5
Q_EPS = 1e-8
GM_N = 172
GM_W = 2048


def build(phases=PHASES, dbg=DBG):
    nc = bass.Bass("TRN2", target_bir_lowering=False, debug=False)

    xT = nc.dram_tensor("xT", [D, S], F32, kind="ExternalInput").ap()
    wqT = nc.dram_tensor("wqT", [D, D], F32, kind="ExternalInput").ap()
    wkT = nc.dram_tensor("wkT", [D, D], F32, kind="ExternalInput").ap()
    wvT = nc.dram_tensor("wvT", [D, D], F32, kind="ExternalInput").ap()
    wgT = nc.dram_tensor("wgT", [D, D], F32, kind="ExternalInput").ap()
    bq = nc.dram_tensor("bq", [128, NDC], F32, kind="ExternalInput").ap()
    bk = nc.dram_tensor("bk", [128, NDC], F32, kind="ExternalInput").ap()
    bv = nc.dram_tensor("bv", [128, NDC], F32, kind="ExternalInput").ap()
    bg = nc.dram_tensor("bg", [128, NDC], F32, kind="ExternalInput").ap()
    w1T = nc.dram_tensor("w1T", [D, DFF], F32, kind="ExternalInput").ap()
    w2T = nc.dram_tensor("w2T", [D, DFF], F32, kind="ExternalInput").ap()
    w3T = nc.dram_tensor("w3T", [DFF, D], F32, kind="ExternalInput").ap()
    cfm = nc.dram_tensor("cfm", [128, NDC], F32, kind="ExternalInput").ap()
    wmod_s = nc.dram_tensor("wmod_s", [D, 1536], F32, kind="ExternalInput").ap()
    maskx = nc.dram_tensor("maskx", [128, 8], F32, kind="ExternalInput").ap()

    outT = nc.dram_tensor("outT", [D, S], F32, kind="ExternalOutput").ap()

    ik = "ExternalOutput" if dbg else "Internal"
    xp_buf = nc.dram_tensor("xp_buf", [D, S], F32, kind=ik).ap()
    xq_buf = nc.dram_tensor("xq_buf", [D, S], F32, kind=ik).ap()
    gaqg_buf = nc.dram_tensor("gaqg_buf", [D, S], BF16, kind=ik).ap()
    hid_buf = nc.dram_tensor("hid_buf", [DFF, S], BF16, kind=ik).ap()
    w3t_buf = nc.dram_tensor("w3t_buf", [DFF, D], BF16, kind=ik).ap()
    ag1_in = nc.dram_tensor("ag1_in", [1, 1536], F32, kind="Internal")
    ag1_out = nc.dram_tensor("ag1_out", [8, 1536], F32, kind="Internal")
    gred_d = nc.dram_tensor("gred_d", [1, 384], F32, kind="Internal")
    ag2_in = nc.dram_tensor("ag2_in", [1, D], F32, kind="Internal")
    ag2_out = nc.dram_tensor("ag2_out", [8, D], F32, kind="Internal")
    if dbg == 2:
        dbg_ps = nc.dram_tensor("dbg_ps", [128, 6 * SGS], F32, kind="ExternalOutput").ap()
        dbg_cum = nc.dram_tensor("dbg_cum", [128, 8 * SGS], F32, kind="ExternalOutput").ap()
        dbg_gacc = nc.dram_tensor("dbg_gacc", [128, GM_N], F32, kind="ExternalOutput").ap()
        dbg_g3 = nc.dram_tensor("dbg_g3", [128, 8], F32, kind="ExternalOutput").ap()
        dbg_g4 = nc.dram_tensor("dbg_g4", [1, 512], F32, kind="ExternalOutput").ap()
        dbg_a1 = nc.dram_tensor("dbg_a1", [128, 4 * SGS], F32, kind="ExternalOutput").ap()
        dbg_wt = nc.dram_tensor("dbg_wt", [128, NDC * 128], BF16, kind="ExternalOutput").ap()
    if dbg:
        dbg_h = nc.dram_tensor("dbg_h", [D, S], BF16, kind="ExternalOutput").ap()
        dbg_mod = nc.dram_tensor("dbg_mod", [128, 6 * NDC], F32, kind="ExternalOutput").ap()
        dbg_total = nc.dram_tensor("dbg_total", [128, NDC], F32, kind="ExternalOutput").ap()
        dbg_pfx = nc.dram_tensor("dbg_pfx", [128, NDC], F32, kind="ExternalOutput").ap()
        dbg_thr = nc.dram_tensor("dbg_thr", [128, 6], F32, kind="ExternalOutput").ap()

    ones_col_f_c = nc.inline_tensor(np.ones((128, 1), np.float32), "ones_col_f_c")
    ones_row_f_c = nc.inline_tensor(np.ones((1, 128), np.float32), "ones_row_f_c")
    ones_col_b_c = nc.inline_tensor(np.ones((128, 1), ml_dtypes.bfloat16), "ones_col_b_c")
    id128_c = nc.inline_tensor(np.eye(128, dtype=np.float32), "id128_c")

    A = nc.alloc_sbuf_tensor
    hbuf = A("hbuf", [128, NDC, S], BF16)
    bq_s = A("bq_s", [128, NDC], F32)
    bk_s = A("bk_s", [128, NDC], F32)
    bv_s = A("bv_s", [128, NDC], F32)
    bg_s = A("bg_s", [128, NDC], F32)
    c_sb = A("c_sb", [128, NDC], F32)
    scs = A("scs", [128, NDC], F32)
    scb = A("scb", [128, NDC], BF16)
    mod_fm = A("mod_fm", [128, 6, NDC], F32)
    ca1 = A("ca1", [128, NDC], F32)
    ca1n = A("ca1n", [128, NDC], F32)
    ga_c = A("ga_c", [128, NDC], F32)
    cb1h = A("cb1h", [128, NDC], F32)
    cb1hn = A("cb1hn", [128, NDC], F32)
    sbh = A("sbh", [128, NDC], F32)
    gb_c = A("gb_c", [128, NDC], F32)
    tmpm = A("tmpm", [128, NDC], F32)
    tmpm2 = A("tmpm2", [128, NDC], F32)
    maskx_s = A("maskx_s", [128, 8], F32)
    Sfm = A("Sfm", [128, 128], F32)
    S128 = A("S128", [128, 128], F32)
    m96 = A("m96", [96, 128], F32)
    t16 = A("t16", [16, 128], F32)
    id128 = A("id128", [128, 128], F32)
    total_fm = A("total_fm", [128, NDC], F32)
    pfx_sb = A("pfx_sb", [128, NDC], F32)
    zvec = A("zvec", [128, 1], F32)
    gacc = A("gacc", [128, GM_N], F32)
    gred = A("gred", [128, 3], F32)
    thr_sb = A("thr_sb", [128, 6], F32)
    mod_row = A("mod_row", [1, 1536], F32)
    rows_sb = A("rows_sb", [1, 1024], F32)
    murow = A("murow", [1, SGS], F32)
    m2row = A("m2row", [1, SGS], F32)
    tmpr = A("tmpr", [1, SGS], F32)
    varp = A("varp", [1, SGS], F32)
    stdr = A("stdr", [1, SGS], F32)
    rstdr = A("rstdr", [1, SGS], F32)
    mrbrow = A("mrbrow", [1, SGS], F32)
    g384 = A("g384", [1, 384], F32)
    gsum = A("gsum", [1, 3], F32)
    grow = A("grow", [1, 3], F32)
    trow = A("trow", [1, 3], F32)
    tnrow = A("tnrow", [1, 3], F32)
    ones_col_f = A("ones_col_f", [128, 1], F32)
    ones_row_f = A("ones_row_f", [1, 128], F32)
    ones_col_b = A("ones_col_b", [128, 1], BF16)
    gm_trash = A("gm_trash", [128, GM_W], BF16)

    SM = nc.alloc_semaphore
    sm_c = SM("sm_c")
    sm_wmod = SM("sm_wmod")
    tm_mod = SM("tm_mod")
    sc_mod = SM("sc_mod")
    sm_ag1 = SM("sm_ag1")
    cc1 = SM("cc1")
    sm_modfm = SM("sm_modfm")
    tm_tr = SM("tm_tr")
    v_mfm = SM("v_mfm")
    tm_tr2 = SM("tm_tr2")
    v_tr2 = SM("v_tr2")
    v_mod = SM("v_mod")
    sm_x0 = [SM("sm_x0a"), SM("sm_x0b")]
    v_sq = SM("v_sq")
    tm_st = SM("tm_st")
    v_evr = SM("v_evr")
    v_rowA = SM("v_rowA")
    sc_sq = SM("sc_sq")
    tm_bc = SM("tm_bc")
    v_h = SM("v_h")
    sm_w = [SM("sm_wa"), SM("sm_wb")]
    tm_a1 = SM("tm_a1")
    sc_a1 = SM("sc_a1")
    v_kq = SM("v_kq")
    v_ep = SM("v_ep")
    sm_x = [SM("sm_xa"), SM("sm_xb")]
    st_g = [SM("st_ga"), SM("st_gb")]
    st_x2 = [SM("st_x2a"), SM("st_x2b")]
    sm_gm = [SM(f"sm_gm{i}") for i in range(4)]
    sc_gm = SM("sc_gm")
    sm_gd1 = SM("sm_gd1")
    sm_gd2 = SM("sm_gd2")
    v_gf = SM("v_gf")
    tm_gf = SM("tm_gf")
    sm_ag2 = SM("sm_ag2")
    cc2 = SM("cc2")
    sm_S = SM("sm_S")
    v_pfx = SM("v_pfx")
    sm_xp = [SM("sm_xpa"), SM("sm_xpb")]
    sm_gq = [SM("sm_gqa"), SM("sm_gqb")]
    v_a2 = SM("v_a2")
    tm_st2 = SM("tm_st2")
    v_evr2 = SM("v_evr2")
    v_row2 = SM("v_row2")
    sc_sq2 = SM("sc_sq2")
    tm_bc2 = SM("tm_bc2")
    v_h2 = SM("v_h2")
    st_xq = SM("st_xq")
    sm_stg = [SM("sm_stga"), SM("sm_stgb")]
    v_f1s = SM("v_f1s")
    tm_f1 = SM("tm_f1")
    sc_f1e = SM("sc_f1e")
    v_f1e = SM("v_f1e")
    st_hid = SM("st_hid")
    st_w3 = SM("st_w3")
    sm_hid = SM("sm_hid")
    sm_w3q = [SM("sm_w3qa"), SM("sm_w3qb")]
    sm_xe = [SM("sm_xea"), SM("sm_xeb")]
    tm_f2 = SM("tm_f2")
    v_f2 = SM("v_f2")
    st_out = [SM("st_outa"), SM("st_outb")]
    st_dbg = SM("st_dbg")

    ph = ExitStack()
    x0_sb = [ph.enter_context(nc.sbuf_tensor(f"x0_{i}", [128, NDC, SGS], F32)) for i in range(2)]
    sqb = ph.enter_context(nc.sbuf_tensor("sqb", [128, NDC, SGS], BF16))
    wmod_sb = ph.enter_context(nc.sbuf_tensor("wmod_sb", [128, NDC, 256], F32))
    ps_mod = ph.enter_context(nc.psum_tensor("ps_mod", [128, 512], F32))
    ps_s0 = ph.enter_context(nc.psum_tensor("ps_s0", [128, 512], F32))
    ps_s1 = ph.enter_context(nc.psum_tensor("ps_s1", [128, 512], F32))
    ps_rstd = ph.enter_context(nc.psum_tensor("ps_rstd", [128, 512], F32))
    ps_mrb = ph.enter_context(nc.psum_tensor("ps_mrb", [128, 512], F32))

    with nc.Block("ph0") as block:

        @block.gpsimd
        def _(g):
            g.dma_start(out=c_sb[:], in_=cfm).then_inc(sm_c, 16)
            g.dma_start(out=maskx_s[:], in_=maskx).then_inc(sm_c, 16)
            g.dma_start(out=bq_s[:], in_=bq).then_inc(sm_c, 16)
            g.dma_start(out=bk_s[:], in_=bk).then_inc(sm_c, 16)
            g.dma_start(out=bv_s[:], in_=bv).then_inc(sm_c, 16)
            g.dma_start(out=bg_s[:], in_=bg).then_inc(sm_c, 16)
            g.dma_start(out=ones_col_f[:], in_=ones_col_f_c.ap()).then_inc(sm_c, 16)
            g.dma_start(out=ones_row_f[:], in_=ones_row_f_c.ap()).then_inc(sm_c, 16)
            g.dma_start(out=ones_col_b[:], in_=ones_col_b_c.ap()).then_inc(sm_c, 16)
            g.dma_start(out=id128[:], in_=id128_c.ap()).then_inc(sm_c, 16)
            for ng in range(6):
                if ng >= 1:
                    g.wait_ge(tm_mod, ng)
                src = wmod_s[:, ng * 256:(ng + 1) * 256].rearrange("(c p) n -> p c n", p=128)
                g.dma_start(out=wmod_sb[:], in_=src).then_inc(sm_wmod, 16)
            g.wait_ge(sc_mod, 8)
            g.dma_start(out=ag1_in.ap(), in_=mod_row[:]).then_inc(sm_ag1, 16)
            g.wait_ge(sm_ag1, 16)
            g.collective_compute(
                "AllGather", ALU.bypass, replica_groups=[list(range(8))],
                ins=[ag1_in.ap()], outs=[ag1_out.ap()],
            ).then_inc(cc1, 1)
            g.wait_ge(cc1, 1)
            g.dma_start(
                out=m96[:],
                in_=bass.AP(ag1_out, 0, [[128, 96], [1, 128]]),
            ).then_inc(sm_modfm, 16)

        @block.scalar
        def _(sc):
            sc.wait_ge(sm_c, 160)
            sc.activation(scs[:], c_sb[:], AF.Silu).then_inc(sc_mod, 1)
            sc.copy(scb[:], scs[:]).then_inc(sc_mod, 1)
            for ng in range(6):
                sc.wait_ge(tm_mod, ng + 1)
                sc.activation(mod_row[0:1, ng * 256:(ng + 1) * 256],
                              ps_mod[0:1, 0:256], AF.Copy).then_inc(sc_mod, 1)
            for gg in range(NSG):
                sc.wait_ge(v_rowA, 2 * gg + 1)
                sc.activation(stdr[:], varp[:], AF.Sqrt).then_inc(sc_sq, 1)

        @block.tensor
        def _(t):
            t.wait_ge(sc_mod, 2)
            for ng in range(6):
                t.wait_ge(sm_wmod, 16 * (ng + 1))
                if ng >= 1:
                    t.wait_ge(sc_mod, 2 + ng)
                for dc in range(NDC):
                    mm = t.matmul(ps_mod[0:1, 0:256], scs[:, dc:dc + 1],
                                  wmod_sb[:, dc, :], start=(dc == 0), stop=(dc == NDC - 1))
                mm.then_inc(tm_mod, 1)
            t.wait_ge(sm_modfm, 16)
            t.wait_ge(sc_mod, 8)
            t.transpose(ps_mod[:, 0:96], m96[:], id128[0:96, 0:96]).then_inc(tm_tr, 1)
            for gg in range(NSG):
                t.wait_ge(sm_x0[gg % 2], 16 * (gg // 2 + 1))
                if gg >= 1:
                    t.wait_ge(v_evr, 2 * gg)
                for dc in range(NDC):
                    t.matmul(ps_s0[0:1, 0:SGS], ones_col_f[:],
                             x0_sb[gg % 2][:, dc, :], start=(dc == 0), stop=(dc == NDC - 1))
                t.wait_ge(v_sq, gg + 1)
                for dc in range(NDC):
                    mm = t.matmul(ps_s1[0:1, 0:SGS], ones_col_b[:],
                                  sqb[:, dc, :], start=(dc == 0), stop=(dc == NDC - 1))
                mm.then_inc(tm_st, 1)
                t.wait_ge(v_rowA, 2 * (gg + 1))
                if gg >= 1:
                    t.wait_ge(v_h, gg)
                t.matmul(ps_rstd[:, 0:SGS], ones_row_f[:], rstdr[:], start=True, stop=True)
                t.matmul(ps_mrb[:, 0:SGS], ones_row_f[:], mrbrow[:],
                         start=True, stop=True).then_inc(tm_bc, 1)

        @block.sync
        def _(s):
            for gg in range(NSG):
                if gg >= 2:
                    s.wait_ge(v_h, gg - 1)
                src = xT[:, gg * SGS:(gg + 1) * SGS].rearrange("(c p) n -> p c n", p=128)
                s.dma_start(out=x0_sb[gg % 2][:], in_=src).then_inc(sm_x0[gg % 2], 16)

        @block.vector
        def _(v):
            v.memset(zvec[:], 0.0)
            v.wait_ge(tm_tr, 1)
            v.tensor_copy(mod_fm[:].rearrange("p a b -> p (a b)"),
                          ps_mod[:, 0:96]).then_inc(v_mfm, 1)
            v.tensor_scalar_add(ca1[:], mod_fm[:, 1, :], 1.0)

            def clamp_gate(src_ap, dst):
                v.tensor_scalar(tmpm[:], src_ap, 0.0, 2.0, ALU.is_ge, ALU.mult)
                v.tensor_scalar_mul(tmpm2[:], src_ap, -1.0)
                v.engine_nop()
                v.tensor_scalar_add(tmpm[:], tmpm[:], -1.0)
                v.tensor_max(tmpm2[:], tmpm2[:], src_ap)
                v.engine_nop()
                v.tensor_scalar_max(tmpm2[:], tmpm2[:], 0.25)
                v.engine_nop()
                v.tensor_mul(dst[:], tmpm[:], tmpm2[:])

            clamp_gate(mod_fm[:, 2, :], ga_c)
            v.tensor_scalar(cb1h[:], mod_fm[:, 4, :], 1.0, 0.5, ALU.add, ALU.mult)
            v.tensor_scalar_mul(sbh[:], mod_fm[:, 3, :], 0.5)
            v.engine_nop().then_inc(v_mod, 1)
            clamp_gate(mod_fm[:, 5, :], gb_c)
            v.engine_nop().then_inc(v_mod, 1)
            for gg in range(NSG):
                v.wait_ge(sm_x0[gg % 2], 16 * (gg // 2 + 1))
                if gg >= 1:
                    v.wait_ge(tm_st, gg)
                for dc in range(NDC):
                    v.tensor_mul(sqb[:, dc, :], x0_sb[gg % 2][:, dc, :], x0_sb[gg % 2][:, dc, :])
                v.engine_nop().then_inc(v_sq, 1)
                v.wait_ge(tm_st, gg + 1)
                v.tensor_copy(rows_sb[0:1, 0:SGS], ps_s0[0:1, 0:SGS])
                v.tensor_copy(rows_sb[0:1, SGS:2 * SGS], ps_s1[0:1, 0:SGS]).then_inc(v_evr, 2)
                v.tensor_scalar_mul(murow[:], rows_sb[0:1, 0:SGS], 1.0 / D)
                v.tensor_scalar_mul(m2row[:], rows_sb[0:1, SGS:2 * SGS], 1.0 / D)
                v.tensor_mul(tmpr[:], murow[:], murow[:])
                v.engine_nop()
                v.tensor_sub(tmpr[:], m2row[:], tmpr[:])
                v.engine_nop()
                v.tensor_scalar_add(varp[:], tmpr[:], LN_EPS).then_inc(v_rowA, 1)
                v.wait_ge(sc_sq, gg + 1)
                if gg >= 1:
                    v.wait_ge(tm_bc, gg)
                v.reciprocal(rstdr[:], stdr[:])
                v.engine_nop()
                v.tensor_mul(mrbrow[:], murow[:], rstdr[:]).then_inc(v_rowA, 1)
                v.wait_ge(tm_bc, gg + 1)
                v.wait_ge(v_mod, 1)
                for dc in range(NDC):
                    xs = x0_sb[gg % 2][:, dc, :]
                    v.tensor_mul(xs, xs, ps_rstd[:, 0:SGS])
                    v.engine_nop()
                    v.tensor_sub(xs, xs, ps_mrb[:, 0:SGS])
                    v.engine_nop()
                    v.tensor_scalar(hbuf[:, dc, gg * SGS:(gg + 1) * SGS],
                                    xs, ca1[:, dc:dc + 1], mod_fm[:, 0, dc:dc + 1],
                                    ALU.mult, ALU.add)
                v.engine_nop().then_inc(v_h, 1)
            if dbg == 2:
                v.tensor_copy(x0_sb[0][:, 0, :], ps_rstd[:, 0:SGS])
                v.tensor_copy(x0_sb[0][:, 1, :], ps_mrb[:, 0:SGS])
                v.tensor_copy(x0_sb[0][0:1, 2, :], rows_sb[0:1, 0:SGS])
                v.tensor_copy(x0_sb[0][0:1, 3, :], mrbrow[0:1, 0:SGS])
                v.tensor_copy(x0_sb[0][0:1, 4, :], murow[0:1, 0:SGS])
                v.tensor_copy(x0_sb[0][0:1, 5, :], rstdr[0:1, 0:SGS])
                v.engine_nop().then_inc(v_h, 1)

        if dbg == 2:
            @block.sync
            def _(s2):
                s2.wait_ge(v_h, 5)
                s2.dma_start(out=dbg_ps, in_=x0_sb[0][:, 0:6, :].rearrange("p a b -> p (a b)")).then_inc(st_dbg, 16)

    ph.close()

    if phases >= 2:
        ph = ExitStack()
        E = ph.enter_context
        wq_t = [E(nc.sbuf_tensor(f"wq_t{i}", [128, NDC, 128], BF16)) for i in range(2)]
        wk_t = [E(nc.sbuf_tensor(f"wk_t{i}", [128, NDC, 128], BF16)) for i in range(2)]
        wv_t = [E(nc.sbuf_tensor(f"wv_t{i}", [128, NDC, 128], BF16)) for i in range(2)]
        wg_t = [E(nc.sbuf_tensor(f"wg_t{i}", [128, NDC, 128], BF16)) for i in range(2)]
        gm_stg = [E(nc.sbuf_tensor(f"gm_stg{i}", [128, GM_W], F32)) for i in range(4)]
        x_sb = [E(nc.sbuf_tensor(f"x_sb{i}", [128, SGS], F32)) for i in range(2)]
        k_sb = [E(nc.sbuf_tensor(f"k_sb{i}", [128, SGS], F32)) for i in range(2)]
        v_sb2 = [E(nc.sbuf_tensor(f"v_sb2{i}", [128, SGS], F32)) for i in range(2)]
        q_sb = [E(nc.sbuf_tensor(f"q_sb{i}", [128, SGS], F32)) for i in range(2)]
        g_sb = [E(nc.sbuf_tensor(f"g_sb{i}", [128, SGS], F32)) for i in range(2)]
        kv_sb = E(nc.sbuf_tensor("kv_sb", [128, SGS], F32))

        cum_sb = [E(nc.sbuf_tensor(f"cum_sb{i}", [128, SGS], F32)) for i in range(2)]
        tt_sb = E(nc.sbuf_tensor("tt_sb", [128, SGS], F32))
        gaqg_t = [E(nc.sbuf_tensor(f"gaqg_t{i}", [128, SGS], BF16)) for i in range(2)]
        k_ps = [E(nc.psum_tensor(f"k_ps{i}", [128, 512], F32)) for i in range(2)]
        v_ps = [E(nc.psum_tensor(f"v_ps{i}", [128, 512], F32)) for i in range(2)]
        q_ps = [E(nc.psum_tensor(f"q_ps{i}", [128, 512], F32)) for i in range(2)]
        g_ps = [E(nc.psum_tensor(f"g_ps{i}", [128, 512], F32)) for i in range(2)]

        wmats = {"q": (wqT, wq_t), "k": (wkT, wk_t), "v": (wvT, wv_t), "g": (wgT, wg_t)}

        with nc.Block("a1") as block:
            if dbg == 2:
                @block.sync
                def _(s):
                    s.wait_ge(sc_a1, 4)
                    s.dma_start(out=dbg_a1[:, 0:SGS], in_=k_sb[0][:]).then_inc(st_dbg, 16)
                    s.dma_start(out=dbg_a1[:, SGS:2*SGS], in_=v_sb2[0][:]).then_inc(st_dbg, 16)
                    s.dma_start(out=dbg_a1[:, 2*SGS:3*SGS], in_=q_sb[0][:]).then_inc(st_dbg, 16)
                    s.dma_start(out=dbg_a1[:, 3*SGS:4*SGS], in_=g_sb[0][:]).then_inc(st_dbg, 16)
                    s.dma_start(out=dbg_wt, in_=wq_t[0][:].rearrange("p a b -> p (a b)")).then_inc(st_dbg, 16)

            @block.gpsimd
            def _(g):
                g.wait_ge(v_h, 4)
                for ec in range(NEC):
                    if ec >= 2:
                        g.wait_ge(tm_a1, 16 * (ec - 1))
                    for nm in ("q", "k", "v", "g"):
                        wap, wt = wmats[nm]
                        src = wap[:, ec * 128:(ec + 1) * 128].rearrange("(c p) e -> p c e", p=128)
                        g.dma_start(out=wt[ec % 2][:], in_=src).then_inc(sm_w[ec % 2], 16)
                if dbg == 2:
                    g.wait_ge(sc_gm, GM_N)
                    g.dma_start(out=dbg_gacc, in_=gacc[:]).then_inc(st_dbg, 16)
                g.wait_ge(v_gf, 1)
                for m in range(3):
                    g.dma_start(out=bass.AP(gred_d, m * 128, [[1, 1], [1, 128]]),
                                in_=gred[:, m:m + 1]).then_inc(sm_gd1, 16)
                g.wait_ge(sm_gd1, 48)
                g.dma_start(out=g384[:], in_=gred_d.ap()).then_inc(sm_gd2, 16)
                if dbg == 2:
                    g.wait_ge(v_gf, 3)
                    g.dma_start(out=dbg_g3[:, 0:3], in_=gred[:]).then_inc(st_dbg, 16)
                    g.dma_start(out=bass.AP(dbg_g3.tensor, 3, [[8, 128], [1, 3]]),
                                in_=bass.AP(gred_d, 0, [[3, 128], [1, 3]])).then_inc(st_dbg, 16)
                    g.dma_start(out=dbg_g4[0:1, 0:384], in_=g384[:]).then_inc(st_dbg, 16)
                    g.dma_start(out=dbg_g4[0:1, 384:387], in_=gsum[:]).then_inc(st_dbg, 16)
                    g.dma_start(out=dbg_g4[0:1, 387:390], in_=trow[:]).then_inc(st_dbg, 16)
                    g.dma_start(out=dbg_g4[0:1, 390:393], in_=tnrow[:]).then_inc(st_dbg, 16)
                if phases >= 3:
                    g.wait_ge(v_tr2, 1)
                    g.dma_start(out=bass.AP(ag2_in, 0, [[128, 16], [1, 128]]),
                                in_=t16[:]).then_inc(sm_ag2, 16)
                    g.wait_ge(sm_ag2, 16)
                    g.collective_compute(
                        "AllGather", ALU.bypass, replica_groups=[list(range(8))],
                        ins=[ag2_in.ap()], outs=[ag2_out.ap()],
                    ).then_inc(cc2, 1)
                    g.wait_ge(cc2, 1)
                    g.dma_start(out=S128[:],
                                in_=bass.AP(ag2_out, 0, [[128, 128], [1, 128]]),
                                ).then_inc(sm_S, 16)

            @block.sync
            def _(s):
                s.wait_ge(v_h, 4)
                for it in range(64):
                    ec, sg = it // 4, it % 4
                    par = it % 2
                    if it >= 2:
                        s.wait_ge(st_x2[par], 16 * (it // 2))
                    s.dma_start(out=x_sb[par][:],
                                in_=xT[ec * 128:(ec + 1) * 128, sg * SGS:(sg + 1) * SGS]
                                ).then_inc(sm_x[par], 16)
                    s.wait_ge(v_ep, it + 1)
                    s.dma_start(out=gaqg_buf[ec * 128:(ec + 1) * 128, sg * SGS:(sg + 1) * SGS],
                                in_=gaqg_t[par][:]).then_inc(st_g[par], 16)
                    s.dma_start(out=xp_buf[ec * 128:(ec + 1) * 128, sg * SGS:(sg + 1) * SGS],
                                in_=x_sb[par][:]).then_inc(st_x2[par], 16)

            @block.tensor
            def _(t):
                t.wait_ge(v_h, 4)
                for it in range(64):
                    ec, sg = it // 4, it % 4
                    if sg == 0:
                        t.wait_ge(sm_w[ec % 2], 64 * (ec // 2 + 1))
                    if it == 2 and dbg == 2:
                        t.wait_ge(st_dbg, 80)
                    if it >= 2:
                        t.wait_ge(v_ep, it - 1)
                        t.wait_ge(sc_a1, 4 * (it - 1))
                    par = it % 2
                    for nm, ps in (("k", k_ps), ("v", v_ps), ("q", q_ps), ("g", g_ps)):
                        wt = wmats[nm][1][ec % 2]
                        for dc in range(NDC):
                            mm = t.matmul(ps[par][:, 0:SGS], wt[:, dc, :],
                                          hbuf[:, dc, sg * SGS:(sg + 1) * SGS],
                                          start=(dc == 0), stop=(dc == NDC - 1))
                        mm.then_inc(tm_a1, 1)
                if phases >= 3:
                    t.wait_ge(v_ep, 64)
                    t.transpose(v_ps[0][0:16, 0:128], total_fm[:], id128[:]).then_inc(tm_tr2, 1)
                t.wait_ge(v_gf, 3)
                t.wait_ge(sc_a1, 4 * 64)
                t.matmul(k_ps[1][:, 0:3], ones_row_f[:], trow[:], start=True, stop=True)
                t.matmul(k_ps[1][:, 3:6], ones_row_f[:], tnrow[:],
                         start=True, stop=True).then_inc(tm_gf, 1)
                if phases >= 3:
                    t.wait_ge(sm_S, 16)
                    t.transpose(q_ps[0][:, 0:128], S128[:], id128[:]).then_inc(tm_tr2, 1)

            @block.scalar
            def _(sc):
                def gm_src(cidx):
                    if cidx < 128:
                        wap = w1T if cidx < 64 else w2T
                        rem = cidx % 64
                        dc, qq = rem // 4, rem % 4
                        return wap[dc * 128:(dc + 1) * 128, qq * 1408:(qq + 1) * 1408], 1408
                    j = cidx - 128
                    return w3T[j * 128:(j + 1) * 128, :], 2048
                sc.wait_ge(v_h, 4)
                for cpre in range(4):
                    src, w = gm_src(cpre)
                    sc.dma_start(out=gm_stg[cpre % 4][:, 0:w], in_=src).then_inc(sm_gm[cpre % 4], 16)
                for it in range(64):
                    ec, sg = it // 4, it % 4
                    par = it % 2
                    if it >= 2:
                        sc.wait_ge(v_ep, it - 1)
                    sc.wait_ge(tm_a1, 4 * it + 1)
                    sc.activation(k_sb[par][:], k_ps[par][:, 0:SGS], AF.Identity,
                                  bias=bk_s[:, ec:ec + 1]).then_inc(sc_a1, 1)
                    sc.wait_ge(tm_a1, 4 * it + 2)
                    sc.activation(v_sb2[par][:], v_ps[par][:, 0:SGS], AF.Identity,
                                  bias=bv_s[:, ec:ec + 1]).then_inc(sc_a1, 1)
                    sc.wait_ge(tm_a1, 4 * it + 3)
                    sc.activation(q_sb[par][:], q_ps[par][:, 0:SGS], AF.Identity,
                                  bias=bq_s[:, ec:ec + 1]).then_inc(sc_a1, 1)
                    sc.wait_ge(tm_a1, 4 * it + 4)
                    sc.activation(g_sb[par][:], g_ps[par][:, 0:SGS], AF.Sigmoid,
                                  bias=bg_s[:, ec:ec + 1]).then_inc(sc_a1, 1)
                    for c in (3 * it, 3 * it + 1, 3 * it + 2):
                        if c >= GM_N:
                            continue
                        w = gm_src(c)[1]
                        sc.wait_ge(sm_gm[c % 4], 16 * (c // 4 + 1))
                        sc.activation(gm_trash[:, 0:w], gm_stg[c % 4][:, 0:w], AF.Abs,
                                      accum_out=gacc[:, c:c + 1]).then_inc(sc_gm, 1)
                        if c + 4 < GM_N:
                            src, w2 = gm_src(c + 4)
                            sc.dma_start(out=gm_stg[(c + 4) % 4][:, 0:w2], in_=src
                                         ).then_inc(sm_gm[(c + 4) % 4], 16)

            @block.vector
            def _(v):
                for it in range(64):
                    ec, sg = it // 4, it % 4
                    par = it % 2
                    if it >= 2:
                        v.wait_ge(st_g[par], 16 * (it // 2))
                        v.wait_ge(st_x2[par], 16 * (it // 2))
                    v.wait_ge(sc_a1, 4 * it + 2)
                    v.tensor_mul(kv_sb[:], k_sb[par][:], v_sb2[par][:])
                    v.engine_nop()
                    init = zvec[:, 0:1] if sg == 0 else cum_sb[1 - par][:, SGS - 1:SGS]
                    v.tensor_tensor_scan(cum_sb[par][:], kv_sb[:], kv_sb[:], init,
                                         ALU.add, ALU.bypass)
                    v.engine_nop()
                    v.wait_ge(sc_a1, 4 * it + 4)
                    v.scalar_tensor_tensor(gaqg_t[par][:], q_sb[par][:], ga_c[:, ec:ec + 1],
                                           g_sb[par][:], ALU.mult, ALU.mult)
                    v.tensor_mul(tt_sb[:], q_sb[par][:], cum_sb[par][:])
                    v.engine_nop()
                    v.tensor_mul(tt_sb[:], tt_sb[:], g_sb[par][:])
                    v.engine_nop()
                    v.wait_ge(sm_x[par], 16 * (it // 2 + 1))
                    v.scalar_tensor_tensor(x_sb[par][:], tt_sb[:], ga_c[:, ec:ec + 1],
                                           x_sb[par][:], ALU.mult, ALU.add)
                    if sg == 3:
                        v.tensor_copy(total_fm[:, ec:ec + 1], cum_sb[par][:, SGS - 1:SGS])
                    if it == 63:
                        v.tensor_copy(tt_sb[:], cum_sb[par][:])
                        v.engine_nop()
                    v.engine_nop().then_inc(v_ep, 1)
                if phases >= 3:
                    v.wait_ge(tm_tr2, 1)
                    v.tensor_copy(t16[:], v_ps[0][0:16, 0:128]).then_inc(v_tr2, 1)
                v.wait_ge(sc_gm, GM_N)
                v.tensor_copy(tt_sb[:], cum_sb[0][:])
                for m, (c0, c1) in enumerate(((0, 64), (64, 128), (128, 172))):
                    v.reduce_sum(gred[:, m:m + 1], gacc[:, c0:c1],
                                 axis=mybir.AxisListType.X)
                v.tensor_copy(tt_sb[:], cum_sb[0][:])
                v.engine_nop().then_inc(v_gf, 1)
                v.wait_ge(sm_gd2, 16)
                nw = float(DFF * D)
                for m in range(3):
                    v.reduce_sum(gsum[0:1, m:m + 1], g384[0:1, m * 128:(m + 1) * 128],
                                 axis=mybir.AxisListType.X)
                v.tensor_copy(tt_sb[:], cum_sb[0][:])
                v.engine_nop()
                v.tensor_scalar(trow[:], gsum[:], 0.5 / nw, 0.5 * Q_EPS,
                                ALU.mult, ALU.add)
                v.tensor_scalar(tnrow[:], gsum[:], -0.5 / nw, -0.5 * Q_EPS,
                                ALU.mult, ALU.add)
                v.tensor_copy(tt_sb[:], cum_sb[0][:])
                v.engine_nop().then_inc(v_gf, 2)
                v.wait_ge(tm_gf, 1)
                v.tensor_copy(thr_sb[:], k_ps[1][:, 0:6]).then_inc(v_gf, 1)
                if phases >= 3:
                    v.wait_ge(tm_tr2, 2)
                    v.tensor_copy(Sfm[:], q_ps[0][:, 0:128])
                    v.engine_nop()
                    v.tensor_scalar_mul(pfx_sb[:], Sfm[:, 0:NDC], maskx_s[:, 0:1])
                    for r in range(1, 8):
                        v.engine_nop()
                        v.scalar_tensor_tensor(pfx_sb[:], Sfm[:, r * NDC:(r + 1) * NDC],
                                               maskx_s[:, r:r + 1],
                                               pfx_sb[:], ALU.mult, ALU.add)
                    v.engine_nop().then_inc(v_pfx, 1)

        ph.close()

    if phases >= 3:
        ph = ExitStack()
        E = ph.enter_context
        x2g = E(nc.sbuf_tensor("x2g", [128, NDC, SGS], F32))
        sqb2 = E(nc.sbuf_tensor("sqb2", [128, NDC, SGS], BF16))
        xp_sb = [E(nc.sbuf_tensor(f"xp_sb{i}", [128, SGS], F32)) for i in range(2)]
        gq_sb = [E(nc.sbuf_tensor(f"gq_sb{i}", [128, SGS], BF16)) for i in range(2)]
        t2a = E(nc.sbuf_tensor("t2a", [128, SGS], F32))
        ps2_s0 = E(nc.psum_tensor("ps2_s0", [128, 512], F32))
        ps2_s1 = E(nc.psum_tensor("ps2_s1", [128, 512], F32))
        ps2_rstd = E(nc.psum_tensor("ps2_rstd", [128, 512], F32))
        ps2_mrb = E(nc.psum_tensor("ps2_mrb", [128, 512], F32))

        with nc.Block("a2") as block:

            @block.scalar
            def _(sc):
                for it2 in range(64):
                    sg, ec = it2 // 16, it2 % 16
                    if it2 == 0:
                        for pp in range(2):
                            sc.wait_ge(st_x2[pp], 16 * 32)
                            sc.wait_ge(st_g[pp], 16 * 32)
                    if it2 >= 2:
                        sc.wait_ge(v_a2, it2 - 1)
                    sc.dma_start(out=xp_sb[it2 % 2][:],
                                 in_=xp_buf[ec * 128:(ec + 1) * 128, sg * SGS:(sg + 1) * SGS]
                                 ).then_inc(sm_xp[it2 % 2], 16)
                    sc.dma_start(out=gq_sb[it2 % 2][:],
                                 in_=gaqg_buf[ec * 128:(ec + 1) * 128, sg * SGS:(sg + 1) * SGS]
                                 ).then_inc(sm_gq[it2 % 2], 16)
                    if ec == 15:
                        sc.wait_ge(v_row2, 2 * sg + 1)
                        sc.activation(stdr[:], varp[:], AF.Sqrt).then_inc(sc_sq2, 1)

            @block.vector
            def _(v):
                for it2 in range(64):
                    sg, ec = it2 // 16, it2 % 16
                    par = it2 % 2
                    v.wait_ge(sm_xp[par], 16 * (it2 // 2 + 1))
                    v.wait_ge(sm_gq[par], 16 * (it2 // 2 + 1))
                    if it2 == 0:
                        v.wait_ge(v_pfx, 1)
                    if ec == 0 and sg >= 1:
                        v.wait_ge(st_xq, 16 * sg)
                        v.wait_ge(tm_st2, sg)
                    v.scalar_tensor_tensor(x2g[:, ec, :], gq_sb[par][:], pfx_sb[:, ec:ec + 1],
                                           xp_sb[par][:], ALU.mult, ALU.add)
                    v.engine_nop()
                    v.tensor_mul(sqb2[:, ec, :], x2g[:, ec, :], x2g[:, ec, :])
                    v.engine_nop().then_inc(v_a2, 1)
                    if ec == 15:
                        v.wait_ge(tm_st2, sg + 1)
                        v.tensor_copy(rows_sb[0:1, 0:SGS], ps2_s0[0:1, 0:SGS])
                        v.tensor_copy(rows_sb[0:1, SGS:2 * SGS], ps2_s1[0:1, 0:SGS]).then_inc(v_evr2, 2)
                        v.tensor_scalar_mul(murow[:], rows_sb[0:1, 0:SGS], 1.0 / D)
                        v.tensor_scalar_mul(m2row[:], rows_sb[0:1, SGS:2 * SGS], 1.0 / D)
                        v.tensor_mul(tmpr[:], murow[:], murow[:])
                        v.engine_nop()
                        v.tensor_sub(tmpr[:], m2row[:], tmpr[:])
                        v.engine_nop()
                        v.tensor_scalar_add(varp[:], tmpr[:], LN_EPS).then_inc(v_row2, 1)
                        v.wait_ge(sc_sq2, sg + 1)
                        if sg >= 1:
                            v.wait_ge(tm_bc2, sg)
                        v.reciprocal(rstdr[:], stdr[:])
                        v.engine_nop()
                        v.tensor_mul(mrbrow[:], murow[:], rstdr[:]).then_inc(v_row2, 1)
                        v.wait_ge(tm_bc2, sg + 1)
                        for dc in range(NDC):
                            v.tensor_mul(t2a[:], x2g[:, dc, :], ps2_rstd[:, 0:SGS])
                            v.engine_nop()
                            v.tensor_sub(t2a[:], t2a[:], ps2_mrb[:, 0:SGS])
                            v.engine_nop()
                            v.tensor_scalar(hbuf[:, dc, sg * SGS:(sg + 1) * SGS],
                                            t2a[:], cb1h[:, dc:dc + 1], sbh[:, dc:dc + 1],
                                            ALU.mult, ALU.add)
                        v.engine_nop().then_inc(v_h2, 1)

            @block.tensor
            def _(t):
                t.wait_ge(tm_gf, 1)
                t.wait_ge(v_gf, 4)
                t.wait_ge(v_pfx, 1)
                for sg in range(NSG):
                    t.wait_ge(v_a2, 16 * (sg + 1))
                    if sg >= 1:
                        t.wait_ge(v_evr2, 2 * sg)
                    for dc in range(NDC):
                        t.matmul(ps2_s0[0:1, 0:SGS], ones_col_f[:], x2g[:, dc, :],
                                 start=(dc == 0), stop=(dc == NDC - 1))
                    for dc in range(NDC):
                        mm = t.matmul(ps2_s1[0:1, 0:SGS], ones_col_b[:], sqb2[:, dc, :],
                                      start=(dc == 0), stop=(dc == NDC - 1))
                    mm.then_inc(tm_st2, 1)
                    t.wait_ge(v_row2, 2 * (sg + 1))
                    if sg >= 1:
                        t.wait_ge(v_h2, sg)
                    t.matmul(ps2_rstd[:, 0:SGS], ones_row_f[:], rstdr[:], start=True, stop=True)
                    t.matmul(ps2_mrb[:, 0:SGS], ones_row_f[:], mrbrow[:],
                             start=True, stop=True).then_inc(tm_bc2, 1)

            @block.sync
            def _(s):
                for sg in range(NSG):
                    s.wait_ge(v_h2, sg + 1)
                    s.dma_start(out=xq_buf[:, sg * SGS:(sg + 1) * SGS]
                                .rearrange("(c p) n -> p c n", p=128),
                                in_=x2g[:]).then_inc(st_xq, 16)

        ph.close()

    if phases >= 4:
        ph = ExitStack()
        E = ph.enter_context
        stg = [E(nc.sbuf_tensor(f"stg{i}", [128, 4096], F32)) for i in range(2)]
        w1q = [E(nc.sbuf_tensor(f"w1q{i}", [128, 4096], BF16)) for i in range(2)]
        w2q = [E(nc.sbuf_tensor(f"w2q{i}", [128, 4096], BF16)) for i in range(2)]
        w3qt = [E(nc.sbuf_tensor(f"w3qt{i}", [128, 4096], BF16)) for i in range(2)]
        btA = E(nc.sbuf_tensor("btA", [128, 4096], BF16))
        btB = E(nc.sbuf_tensor("btB", [128, 4096], BF16))
        hid_t = E(nc.sbuf_tensor("hid_t", [128, 2, S], BF16))
        s1_sb = [E(nc.sbuf_tensor(f"s1_sb{i}", [128, SGS], F32)) for i in range(2)]
        psA = [E(nc.psum_tensor(f"psA{i}", [128, 512], F32)) for i in range(2)]
        psB = [E(nc.psum_tensor(f"psB{i}", [128, 512], F32)) for i in range(2)]

        with nc.Block("f1") as block:

            def stg_src(k):
                js, j = k // 3, k % 3
                if j == 0:
                    return w1T[:, js * JSS:(js + 1) * JSS].rearrange("(c p) n -> p c n", p=128)
                if j == 1:
                    return w2T[:, js * JSS:(js + 1) * JSS].rearrange("(c p) n -> p c n", p=128)
                return w3T[js * JSS:(js + 1) * JSS, :].rearrange("(c p) n -> p c n", p=128)

            @block.sync
            def _(s):
                for js in range(NJS):
                    for j in range(3):
                        k = 3 * js + j
                        if k == 0:
                            s.wait_ge(st_xq, 16 * 4)
                            s.wait_ge(v_h2, 4)
                        if k >= 2:
                            s.wait_ge(v_f1s, k - 1)
                        s.dma_start(out=stg[k % 2][:].rearrange("p (c n) -> p c n",
                                                                c=(16 if j < 2 else 2)),
                                    in_=stg_src(k)).then_inc(sm_stg[k % 2], 16)
                    if js >= 1:
                        s.wait_ge(v_f1s, 3 * (js - 1) + 3)
                        s.dma_start(out=w3t_buf[(js - 1) * JSS:js * JSS, :]
                                    .rearrange("(c p) n -> p c n", p=128),
                                    in_=w3qt[(js - 1) % 2][:].rearrange("p (c n) -> p c n", c=2)
                                    ).then_inc(st_w3, 16)
                        s.wait_ge(v_f1e, 8 * js)
                        s.dma_start(out=hid_buf[(js - 1) * JSS:js * JSS, :]
                                    .rearrange("(c p) n -> p c n", p=128),
                                    in_=hid_t[:]).then_inc(st_hid, 16)
                s.wait_ge(v_f1s, 3 * NJS)
                s.dma_start(out=w3t_buf[(NJS - 1) * JSS:NJS * JSS, :]
                            .rearrange("(c p) n -> p c n", p=128),
                            in_=w3qt[(NJS - 1) % 2][:].rearrange("p (c n) -> p c n", c=2)
                            ).then_inc(st_w3, 16)
                s.wait_ge(v_f1e, 8 * NJS)
                s.dma_start(out=hid_buf[(NJS - 1) * JSS:NJS * JSS, :]
                            .rearrange("(c p) n -> p c n", p=128),
                            in_=hid_t[:]).then_inc(st_hid, 16)

            @block.vector
            def _(v):
                for js in range(NJS):
                    for j, dstl in ((0, w1q), (1, w2q), (2, w3qt)):
                        k = 3 * js + j
                        v.wait_ge(sm_stg[k % 2], 16 * (k // 2 + 1))
                        if j == 2 and js >= 2:
                            v.wait_ge(st_w3, 16 * (js - 1))
                        if j < 2 and js >= 2:
                            v.wait_ge(tm_f1, 16 * (js - 1))
                        v.tensor_scalar(btA[:], stg[k % 2][:], thr_sb[:, j:j + 1], 2.0,
                                        ALU.is_gt, ALU.mult)
                        v.tensor_scalar(btB[:], stg[k % 2][:], thr_sb[:, 3 + j:4 + j], 2.0,
                                        ALU.is_lt, ALU.mult)
                        v.engine_nop()
                        v.tensor_sub(dstl[js % 2][:], btA[:], btB[:])
                        v.tensor_copy(btA[:, 0:512], btB[:, 0:512])
                        v.engine_nop().then_inc(v_f1s, 1)
                    for gidx in range(8):
                        sg, jc = gidx // 2, gidx % 2
                        G = js * 8 + gidx
                        par = G % 2
                        v.wait_ge(tm_f1, 2 * G + 2)
                        v.wait_ge(sc_f1e, G + 1)
                        if js >= 1 and gidx == 0:
                            v.wait_ge(st_hid, 16 * (js - 1))
                        v.scalar_tensor_tensor(hid_t[:, jc, sg * SGS:(sg + 1) * SGS],
                                               psB[par][:, 0:SGS], 0.5, s1_sb[par][:],
                                               ALU.mult, ALU.mult)
                        v.engine_nop().then_inc(v_f1e, 1)

            @block.tensor
            def _(t):
                for js in range(NJS):
                    for gidx in range(8):
                        sg, jc = gidx // 2, gidx % 2
                        G = js * 8 + gidx
                        par = G % 2
                        if gidx == 0:
                            t.wait_ge(v_f1s, 3 * js + 2)
                        if G >= 2:
                            t.wait_ge(v_f1e, G - 1)
                        for dc in range(NDC):
                            mm = t.matmul(psA[par][:, 0:SGS],
                                          w1q[js % 2][:, dc * JSS + jc * 128:dc * JSS + (jc + 1) * 128],
                                          hbuf[:, dc, sg * SGS:(sg + 1) * SGS],
                                          start=(dc == 0), stop=(dc == NDC - 1))
                        mm.then_inc(tm_f1, 1)
                        for dc in range(NDC):
                            mm = t.matmul(psB[par][:, 0:SGS],
                                          w2q[js % 2][:, dc * JSS + jc * 128:dc * JSS + (jc + 1) * 128],
                                          hbuf[:, dc, sg * SGS:(sg + 1) * SGS],
                                          start=(dc == 0), stop=(dc == NDC - 1))
                        mm.then_inc(tm_f1, 1)

            @block.scalar
            def _(sc):
                for js in range(NJS):
                    for gidx in range(8):
                        G = js * 8 + gidx
                        par = G % 2
                        sc.wait_ge(tm_f1, 2 * G + 1)
                        if G >= 2:
                            sc.wait_ge(v_f1e, G - 1)
                        sc.activation(s1_sb[par][:], psA[par][:, 0:SGS], AF.Sigmoid
                                      ).then_inc(sc_f1e, 1)

        ph.close()

    if phases >= 5:
        ph = ExitStack()
        E = ph.enter_context
        hidF = E(nc.sbuf_tensor("hidF", [128, NJC, SGS], BF16))
        w3F = [E(nc.sbuf_tensor(f"w3F{i}", [128, NJC, 256], BF16)) for i in range(2)]
        xe_sb = [E(nc.sbuf_tensor(f"xe_sb{i}", [128, SGS], F32)) for i in range(2)]
        gbx_sb = [E(nc.sbuf_tensor(f"gbx_sb{i}", [128, SGS], F32)) for i in range(2)]
        out_sb = [E(nc.sbuf_tensor(f"out_sb{i}", [128, SGS], F32)) for i in range(2)]
        ps3 = [E(nc.psum_tensor(f"ps3_{i}", [128, 512], F32)) for i in range(2)]
        sc_f2 = nc.alloc_semaphore("sc_f2")

        with nc.Block("f2") as block:

            @block.gpsimd
            def _(g):
                g.wait_ge(st_hid, 16 * NJS)
                g.wait_ge(st_w3, 16 * NJS)
                for sg in range(NSG):
                    if sg >= 1:
                        g.wait_ge(v_f2, 16 * sg)
                    g.dma_start(out=hidF[:],
                                in_=hid_buf[:, sg * SGS:(sg + 1) * SGS]
                                .rearrange("(c p) n -> p c n", p=128)).then_inc(sm_hid, 16)
                    for eg in range(8):
                        gi = sg * 8 + eg
                        if gi >= 2:
                            g.wait_ge(tm_f2, 2 * (gi - 1))
                        g.dma_start(out=w3F[eg % 2][:],
                                    in_=w3t_buf[:, eg * 256:(eg + 1) * 256]
                                    .rearrange("(c p) n -> p c n", p=128)).then_inc(sm_w3q[gi % 2], 16)

            @block.sync
            def _(s):
                for i in range(64):
                    sg, eg, ec2 = i // 16, (i // 2) % 8, i % 2
                    if i == 0:
                        s.wait_ge(v_f1e, 8 * NJS)
                        s.wait_ge(st_hid, 16 * NJS)
                    if i >= 2:
                        s.wait_ge(v_f2, i - 1)
                    ecg = eg * 2 + ec2
                    s.dma_start(out=xe_sb[i % 2][:],
                                in_=xq_buf[ecg * 128:(ecg + 1) * 128, sg * SGS:(sg + 1) * SGS]
                                ).then_inc(sm_xe[i % 2], 16)
                    s.wait_ge(v_f2, i + 1)
                    s.dma_start(out=outT[ecg * 128:(ecg + 1) * 128, sg * SGS:(sg + 1) * SGS],
                                in_=out_sb[i % 2][:]).then_inc(st_out[i % 2], 16)
                s.wait_ge(st_out[0], 16 * 32)
                s.wait_ge(st_out[1], 16 * 32)

            @block.tensor
            def _(t):
                for i in range(64):
                    sg, eg, ec2 = i // 16, (i // 2) % 8, i % 2
                    gi = sg * 8 + eg
                    if ec2 == 0:
                        t.wait_ge(sm_hid, 16 * (sg + 1))
                        t.wait_ge(sm_w3q[gi % 2], 16 * (gi // 2 + 1))
                    if i >= 2:
                        t.wait_ge(sc_f2, i - 1)
                    for jc in range(NJC):
                        mm = t.matmul(ps3[i % 2][:, 0:SGS],
                                      w3F[eg % 2][:, jc, ec2 * 128:(ec2 + 1) * 128],
                                      hidF[:, jc, :], start=(jc == 0), stop=(jc == NJC - 1))
                    mm.then_inc(tm_f2, 1)

            @block.scalar
            def _(sc):
                for i in range(64):
                    sg, eg, ec2 = i // 16, (i // 2) % 8, i % 2
                    ecg = eg * 2 + ec2
                    sc.wait_ge(tm_f2, i + 1)
                    if i >= 2:
                        sc.wait_ge(v_f2, i - 1)
                    sc.activation(gbx_sb[i % 2][:], ps3[i % 2][:, 0:SGS], AF.Identity,
                                  scale=gb_c[:, ecg:ecg + 1]).then_inc(sc_f2, 1)

            @block.vector
            def _(v):
                for i in range(64):
                    sg, eg, ec2 = i // 16, (i // 2) % 8, i % 2
                    v.wait_ge(sc_f2, i + 1)
                    v.wait_ge(sm_xe[i % 2], 16 * (i // 2 + 1))
                    if i >= 2:
                        v.wait_ge(st_out[i % 2], 16 * (i // 2))
                    v.tensor_add(out_sb[i % 2][:], gbx_sb[i % 2][:], xe_sb[i % 2][:])
                    v.engine_nop().then_inc(v_f2, 1)

        ph.close()

    with nc.Block("fin") as block:

        @block.sync
        def _(s):
            if dbg:
                if phases <= 2:
                    s.wait_ge(v_h, 4)
                    for gg2 in range(4):
                        s.dma_start(out=dbg_h[:, gg2 * SGS:(gg2 + 1) * SGS]
                                    .rearrange("(c p) n -> p c n", p=128),
                                    in_=hbuf[:, :, gg2 * SGS:(gg2 + 1) * SGS]).then_inc(st_dbg, 16)
                s.wait_ge(v_mod, 2)
                s.dma_start(out=dbg_mod, in_=mod_fm[:].rearrange("p a b -> p (a b)")
                            ).then_inc(st_dbg, 16)
                if phases >= 2:
                    s.wait_ge(v_ep, 64)
                    s.dma_start(out=dbg_total, in_=total_fm[:]).then_inc(st_dbg, 16)
                    s.wait_ge(v_gf, 4)
                    s.dma_start(out=dbg_thr, in_=thr_sb[:]).then_inc(st_dbg, 16)
                if phases >= 3:
                    s.wait_ge(v_pfx, 1)
                    s.dma_start(out=dbg_pfx, in_=pfx_sb[:]).then_inc(st_dbg, 16)
                    if phases <= 4:
                        s.wait_ge(v_h2, 4)
                        for gg2 in range(4):
                            s.dma_start(out=dbg_h[:, gg2 * SGS:(gg2 + 1) * SGS]
                                        .rearrange("(c p) n -> p c n", p=128),
                                        in_=hbuf[:, :, gg2 * SGS:(gg2 + 1) * SGS]).then_inc(st_dbg, 16)

        @block.gpsimd
        def _(g):
            if phases >= 5:
                g.wait_ge(st_out[0], 16 * 32)
                g.wait_ge(st_out[1], 16 * 32)
            if phases >= 4:
                g.wait_ge(st_hid, 16 * NJS)
                g.wait_ge(st_w3, 16 * NJS)
            if phases >= 3:
                g.wait_ge(st_xq, 16 * 4)
            if phases >= 2:
                for pp in range(2):
                    g.wait_ge(st_x2[pp], 16 * 32)
                    g.wait_ge(st_g[pp], 16 * 32)
                g.wait_ge(v_gf, 4)
            g.wait_ge(v_h, 4)
            if dbg:
                ndbg = 1
                if phases <= 2:
                    ndbg += 4
                if phases >= 2:
                    ndbg += 2
                if phases >= 3:
                    ndbg += 1
                    if phases <= 4:
                        ndbg += 4
                g.wait_ge(st_dbg, 16 * ndbg)

    return nc


_CACHED = {}


def _get_nc(phases=PHASES, dbg=DBG):
    key = (phases, dbg)
    if key not in _CACHED:
        _CACHED[key] = build(phases, dbg)
    return _CACHED[key]


def make_in_maps(L0_attn_x, L0_attn_Wq, L0_attn_bq, L0_attn_Wk, L0_attn_bk,
                 L0_attn_Wv, L0_attn_bv, L0_attn_Wg, L0_attn_bg,
                 L0_ffn_w1, L0_ffn_w2, L0_ffn_w3, L0_c, L0_wmod):
    f = np.float32
    asn = np.ascontiguousarray

    def fm_vec(b):
        return asn(np.asarray(b, f).reshape(16, 128).T)

    wqT = asn(np.asarray(L0_attn_Wq, f).T)
    wkT = asn(np.asarray(L0_attn_Wk, f).T)
    wvT = asn(np.asarray(L0_attn_Wv, f).T)
    wgT = asn(np.asarray(L0_attn_Wg, f).T)
    w1T = asn(np.asarray(L0_ffn_w1, f).T)
    w2T = asn(np.asarray(L0_ffn_w2, f).T)
    w3T = asn(np.asarray(L0_ffn_w3, f).T)
    bqv = fm_vec(L0_attn_bq)
    bkv = fm_vec(L0_attn_bk)
    bvv = fm_vec(L0_attn_bv)
    bgv = fm_vec(L0_attn_bg)
    cv = fm_vec(np.asarray(L0_c, f).reshape(-1))
    wmod = np.asarray(L0_wmod, f)
    x = np.asarray(L0_attn_x, f).reshape(16384, 2048)

    in_maps = []
    for r in range(8):
        mask = np.zeros((128, 8), f)
        mask[:, :r] = 1.0
        in_maps.append({
            "xT": asn(x[r * 2048:(r + 1) * 2048, :].T),
            "wqT": wqT, "wkT": wkT, "wvT": wvT, "wgT": wgT,
            "bq": bqv, "bk": bkv, "bv": bvv, "bg": bgv,
            "w1T": w1T, "w2T": w2T, "w3T": w3T,
            "cfm": cv,
            "wmod_s": asn(wmod[:, r * 1536:(r + 1) * 1536]),
            "maskx": mask,
        })
    return in_maps


def kernel(L0_attn_x, L0_attn_Wq, L0_attn_bq, L0_attn_Wk, L0_attn_bk,
           L0_attn_Wv, L0_attn_bv, L0_attn_Wg, L0_attn_bg,
           L0_ffn_w1, L0_ffn_w2, L0_ffn_w3, L0_c, L0_wmod):
    in_maps = make_in_maps(L0_attn_x, L0_attn_Wq, L0_attn_bq, L0_attn_Wk, L0_attn_bk,
                           L0_attn_Wv, L0_attn_bv, L0_attn_Wg, L0_attn_bg,
                           L0_ffn_w1, L0_ffn_w2, L0_ffn_w3, L0_c, L0_wmod)
    nc = _get_nc()
    res = run_bass_kernel_spmd(nc, in_maps, core_ids=list(range(8)))
    out = np.empty((1, 16384, 2048), np.float32)
    for r in range(8):
        out[0, r * 2048:(r + 1) * 2048, :] = res.results[r]["outT"].T
    return out
